# version 09
# baseline (speedup 1.0000x reference)
import contextlib
import numpy as np
import ml_dtypes
import concourse.bass as bass
import concourse.mybir as mybir
from concourse.bass_utils import run_bass_kernel_spmd

F32 = mybir.dt.float32
BF16 = mybir.dt.bfloat16
AF = mybir.ActivationFunctionType
ALU = mybir.AluOpType
AX = mybir.AxisListType

D = 1024
SEQ = 8192
TOK = 2048
NQB = 16
OWN_QB = 4
NPAIR = 8
EXPERTS = 8
EH = 2816
EHC = 22
SHH = 1536
SHC = 12
NQTR = 4
EPS = 1e-5

ENGS = ("tensor", "vector", "scalar", "gpsimd", "sync")
P3_SUB = 5
NONE = (None, 0)


class Stream:
    __slots__ = ("name", "step", "count", "sem")

    def __init__(self, name, step=1):
        self.name, self.step, self.count = name, step, 0
        self.sem = None

    @property
    def now(self):
        return (self, self.count)


class Plan:
    def __init__(self):
        self.ops = {e: [] for e in ENGS}
        self.streams = []

    def stream(self, name, step=1):
        s = Stream(name, step)
        self.streams.append(s)
        return s

    def op(self, eng, fn, waits=(), inc=None):
        ws = []
        for w in waits:
            if w is None:
                continue
            s, c = w
            if s is not None and c > 0:
                ws.append((s, c))
        if inc is not None:
            inc.count += inc.step
        self.ops[eng].append((fn, ws, inc))

    def replay(self, ename, eng):
        waited = {}
        for fn, ws, inc in self.ops[ename]:
            for s, c in ws:
                if waited.get(s.name, 0) < c:
                    eng.wait_ge(s.sem, c)
                    waited[s.name] = c
            if fn is None:
                continue
            inst = fn(eng)
            if inc is not None:
                inst.then_inc(inc.sem, inc.step)


def build_nc(stage=4, debug=False):
    nc = bass.Bass("TRN2", target_bir_lowering=False)

    xT = nc.dram_tensor("xT", [D, SEQ], F32, kind="ExternalInput").ap()
    x_own = nc.dram_tensor("x_own", [TOK, D], F32, kind="ExternalInput").ap()
    wqT = nc.dram_tensor("wqT", [D, D], F32, kind="ExternalInput").ap()
    wkT = nc.dram_tensor("wkT", [D, D], F32, kind="ExternalInput").ap()
    wvT = nc.dram_tensor("wvT", [D, D], F32, kind="ExternalInput").ap()
    woT = nc.dram_tensor("woT", [D, D], F32, kind="ExternalInput").ap()
    gwT = nc.dram_tensor("gwT", [D, EXPERTS], F32, kind="ExternalInput").ap()
    w1 = nc.dram_tensor("w1", [EXPERTS, D, EH], F32, kind="ExternalInput").ap()
    w2 = nc.dram_tensor("w2", [EXPERTS, EH, D], F32, kind="ExternalInput").ap()
    w3 = nc.dram_tensor("w3", [EXPERTS, D, EH], F32, kind="ExternalInput").ap()
    sw1 = nc.dram_tensor("sw1", [D, SHH], F32, kind="ExternalInput").ap()
    sw2 = nc.dram_tensor("sw2", [SHH, D], F32, kind="ExternalInput").ap()
    sw3 = nc.dram_tensor("sw3", [D, SHH], F32, kind="ExternalInput").ap()
    ones_in = nc.dram_tensor("ones_in", [128, 128], BF16, kind="ExternalInput").ap()
    identf_in = nc.dram_tensor("identf_in", [128, 128], F32, kind="ExternalInput").ap()

    scratch_kind = "ExternalOutput" if (debug and stage <= 2) else "Internal"
    kT_hbm = nc.dram_tensor("kT_hbm", [D, SEQ], BF16, kind=scratch_kind).ap()
    v_hbm = nc.dram_tensor("v_hbm", [NPAIR, 128, SEQ // 128, 132], BF16,
                           kind=scratch_kind).ap()
    x2_kind = "ExternalOutput" if (debug and stage == 3) else "Internal"
    x2_hbm = nc.dram_tensor("x2_hbm", [TOK, D], F32, kind=x2_kind).ap()

    out = None
    if stage >= 4:
        out = nc.dram_tensor("out", [TOK, D], F32, kind="ExternalOutput").ap()
    dbg = {}
    if debug:
        if stage <= 2:
            dbg["q"] = nc.dram_tensor("dbg_q", [D, TOK], F32, kind="ExternalOutput").ap()
        if stage == 2:
            dbg["O"] = nc.dram_tensor("dbg_O", [D, TOK], F32, kind="ExternalOutput").ap()
            dbg["l"] = nc.dram_tensor("dbg_l", [NPAIR, OWN_QB, 1024], F32, kind="ExternalOutput").ap()
        if stage == 3:
            dbg["xh2T"] = nc.dram_tensor("dbg_xh2T", [D, TOK], F32, kind="ExternalOutput").ap()
            dbg["comb"] = nc.dram_tensor("dbg_comb", [TOK // 128, 128, EXPERTS], F32,
                                         kind="ExternalOutput").ap()
        if stage == 4:
            dbg["hhat"] = nc.dram_tensor("dbg_hhat", [128, EHC * 1024], BF16,
                                         kind="ExternalOutput").ap()
            dbg["yaccE"] = nc.dram_tensor("dbg_yaccE", [128, 8 * D], F32,
                                          kind="ExternalOutput").ap()
            dbg["yaccS"] = nc.dram_tensor("dbg_yaccS", [128, 8 * D], F32,
                                          kind="ExternalOutput").ap()

    plan = Plan()
    pe = plan.stream("pe")
    vch = plan.stream("vch")
    sch = plan.stream("sch")
    gq = plan.stream("gq", 16)
    sq = plan.stream("sq", 16)

    def T(fn, waits=(), inc=None):
        plan.op("tensor", fn, waits, inc)

    def V(fn, waits=()):
        plan.op("vector", fn, list(waits) + [vch.now], inc=vch)
        return vch.now

    def S(fn, waits=()):
        plan.op("scalar", fn, list(waits) + [sch.now], inc=sch)
        return sch.now

    def G(fn, waits=()):
        plan.op("gpsimd", fn, waits, inc=gq)
        return gq.now

    def SY(fn, waits=()):
        plan.op("sync", fn, waits, inc=sq)
        return sq.now

    es = contextlib.ExitStack()
    with es:
        NB16 = 73408
        NF32 = 16000
        ab = es.enter_context(nc.sbuf_tensor("AB", [128, NB16], BF16))
        af = es.enter_context(nc.sbuf_tensor("AF", [128, NF32], F32))

        off = [0]

        def alloc(n):
            o = (off[0] + 1) // 2 * 2
            off[0] = o + n
            assert off[0] <= NB16
            return o

        o_ones = alloc(128)
        o_ident = alloc(0)
        o_woT = alloc(8 * D)
        o_gwT = alloc(8 * 8)
        o_qT = alloc(8 * TOK)
        o_xh2T = alloc(8 * TOK)
        o_O = alloc(8 * TOK)

        p1o = [o_xh2T]

        def alloc_p1(n):
            o = (p1o[0] + 1) // 2 * 2
            p1o[0] = o + n
            assert p1o[0] <= NB16, p1o[0]
            return o

        o_wqT = alloc_p1(8 * D)
        o_wkT = alloc_p1(8 * D)
        o_wvT = alloc_p1(8 * D)
        o_xb = [alloc_p1(8 * 512) for _ in range(2)]
        o_xh = [alloc_p1(8 * 512) for _ in range(2)]
        o_sq1 = alloc_p1(4 * 512)
        o_ktmp = alloc_p1(512)
        o_vsb = [alloc_p1(16 * 66) for _ in range(2)]
        o_rstdb = alloc_p1(512)
        o_mrstdb = alloc_p1(512)
        o_rowsbf = alloc_p1(2 * 512)

        p2a = [o_xh2T]

        def alloc_p2a(n):
            o = (p2a[0] + 1) // 2 * 2
            p2a[0] = o + n
            assert p2a[0] <= o_O, p2a[0]
            return o

        o_kth = [alloc_p2a(32 * 128) for _ in range(3)]
        o_PTA = [alloc_p2a(1024) for _ in range(2)]
        o_PTB = [alloc_p2a(1024) for _ in range(2)]
        p2b = [o_O + 8 * TOK]

        def alloc_p2b(n):
            o = (p2b[0] + 1) // 2 * 2
            p2b[0] = o + n
            assert p2b[0] <= NB16, p2b[0]
            return o

        o_vh = [alloc_p2b(32 * 132) for _ in range(3)]
        o_recip = alloc_p2b(1024)
        o_bA = alloc_p2b(512)
        o_bB = alloc_p2b(512)
        o_stgB = [alloc_p2b(512) for _ in range(2)]

        o_sq3 = o_vh[0]

        o_hh = (o_woT + 1) // 2 * 2
        assert o_hh + EHC * 1024 <= o_xh2T, (o_hh, o_xh2T)
        p4b = [o_O]

        def alloc_p4b(n):
            o = (p4b[0] + 1) // 2 * 2
            p4b[0] = o + n
            assert p4b[0] <= NB16, p4b[0]
            return o

        o_w1c = [alloc_p4b(8 * 256) for _ in range(2)]
        o_w3c = [alloc_p4b(8 * 256) for _ in range(2)]
        o_w2h = [alloc_p4b(11 * D) for _ in range(2)]
        o_stile = [alloc_p4b(512) for _ in range(2)]

        fo = [0]

        def falloc(n):
            o = fo[0]
            fo[0] = o + n
            assert fo[0] <= NF32
            return o

        o_yacc = falloc(8 * D)
        o_fC = o_yacc
        o_rows = o_yacc + 2 * D
        o_lrow = o_yacc + 5 * D
        o_fA = falloc(2 * D)
        o_fB = falloc(2 * D)

        ident_f = es.enter_context(nc.sbuf_tensor("identf", [128, 128], F32))
        cols = es.enter_context(nc.sbuf_tensor("cols", [128, 16], F32))
        t8a = es.enter_context(nc.sbuf_tensor("t8a", [128, 8], F32))
        t8b = es.enter_context(nc.sbuf_tensor("t8b", [128, 8], F32))
        t8c = es.enter_context(nc.sbuf_tensor("t8c", [128, 8], F32))
        comb_sb = es.enter_context(nc.sbuf_tensor("comb", [128, 16 * 8], F32))

        PA = es.enter_context(nc.psum_tensor("PA", [128, 1024], F32))
        PB = es.enter_context(nc.psum_tensor("PB", [128, 1024], F32))
        PC = es.enter_context(nc.psum_tensor("PC", [128, 1024], F32))
        PEx = es.enter_context(nc.psum_tensor("PEx", [128, 1024], F32))

        def bv(o, n):
            return ab[:, o:o + n]

        ONES = bv(o_ones, 128)

        G(lambda e, d=ONES, s=ones_in: e.dma_start(d, s))
        G(lambda e, d=ident_f[:, :], s=identf_in: e.dma_start(d, s))
        for (o, src, n) in ((o_woT, woT, D), (o_gwT, gwT, 8),
                            (o_wqT, wqT, D), (o_wkT, wkT, D), (o_wvT, wvT, D)):
            d = bv(o, 8 * n).rearrange("p (c n) -> p c n", c=8)
            s = src.rearrange("(c p) n -> p c n", p=128)
            G(lambda e, d=d, s=s: e.dma_start(d, s))
        prolog_gq = gq.now
        for b in range(2):
            d = bv(o_vsb[b], 16 * 66).rearrange("p (s j) -> p s j", j=66)[:, :, 64:66]
            V(lambda e, d=d: e.memset(d, 1.0))
        prolog_v = vch.now

        xb_war = [NONE, NONE]
        xh_war = [NONE, NONE]
        ktmp_war = NONE
        vsb_war = [NONE, NONE]
        qk_war = [NONE, NONE]
        v_war = [NONE, NONE]
        pb_war = NONE

        xb_war = [[], []]
        for g in range(NQB):
            bi = g % 2
            xbv = bv(o_xb[bi], 8 * 512).rearrange("p (c t) -> p c t", c=8)
            src = xT.rearrange("(c p) t -> p c t", p=128)[:, :, g * 512:(g + 1) * 512]
            ld = G(lambda e, d=xbv, s=src: e.dma_start(d, s), waits=xb_war[bi])

            def xbc(kc, bi=bi):
                return bv(o_xb[bi] + kc * 512, 512)

            def xhc(kc, bi=bi):
                return bv(o_xh[bi] + kc * 512, 512)

            for kc in range(8):
                T(lambda e, r=xbc(kc), st=(kc == 0), sp=(kc == 7): e.matmul(
                    PB[0:1, 0:512], ONES[:, 0:1], r, start=st, stop=sp),
                  waits=[ld, pb_war] if kc == 0 else [],
                  inc=pe if kc == 7 else None)
            sq_last = None
            for half in range(2):
                sw = [ld] + ([pe.now] if half == 1 else [])
                for j in range(4):
                    kc = half * 4 + j
                    S(lambda e, d=bv(o_sq1 + j * 512, 512), s_=xbc(kc): e.activation(
                        d, s_, AF.Square), waits=sw if j == 0 else [])
                sqr = sch.now
                sq_last = sqr
                for j in range(4):
                    kc = half * 4 + j
                    T(lambda e, r=bv(o_sq1 + j * 512, 512), st=(kc == 0), sp=(kc == 7): e.matmul(
                        PB[0:1, 512:1024], ONES[:, 0:1], r, start=st, stop=sp),
                      waits=[sqr] if j == 0 else [],
                      inc=pe if kc == 7 else None)
            stats_done = pe.now

            R_m, R_t1, R_var, R_sd, R_rstd, R_mr = (
                af[0:1, o_rows + i * 512:o_rows + (i + 1) * 512] for i in range(6))
            V(lambda e: e.tensor_scalar(R_m, PB[0:1, 0:512], 1.0 / D, None, ALU.mult),
              waits=[stats_done])
            V(lambda e: e.tensor_scalar(R_t1, PB[0:1, 512:1024], 1.0 / D, None, ALU.mult))
            V(lambda e: e.tensor_mul(R_var, R_m, R_m))
            V(lambda e: e.tensor_sub(R_var, R_t1, R_var))
            V(lambda e: e.tensor_scalar(R_var, R_var, float(EPS), None, ALU.add))
            S(lambda e: e.activation(R_sd, R_var, AF.Sqrt), waits=[vch.now])
            V(lambda e: e.reciprocal(R_rstd, R_sd), waits=[sch.now])
            V(lambda e: e.tensor_mul(R_mr, R_m, R_rstd))
            V(lambda e: e.tensor_copy(ab[0:1, o_rowsbf:o_rowsbf + 512], R_rstd))
            rows_ready = V(lambda e: e.tensor_copy(ab[0:1, o_rowsbf + 512:o_rowsbf + 1024], R_mr))
            T(lambda e: e.matmul(PB[:, 0:512], ONES[0:1, :], ab[0:1, o_rowsbf:o_rowsbf + 512],
                                 start=True, stop=True),
              waits=[rows_ready, pb_war], inc=pe)
            T(lambda e: e.matmul(PB[:, 512:1024], ONES[0:1, :], ab[0:1, o_rowsbf + 512:o_rowsbf + 1024],
                                 start=True, stop=True), inc=pe)
            bc_done = pe.now
            V(lambda e: e.tensor_copy(bv(o_rstdb, 512), PB[:, 0:512]), waits=[bc_done])
            pb_war = V(lambda e: e.tensor_copy(bv(o_mrstdb, 512), PB[:, 512:1024]))

            for kc in range(8):
                V(lambda e, d=xhc(kc), s_=xbc(kc): e.tensor_mul(d, s_, bv(o_rstdb, 512)),
                  waits=[xh_war[bi]] if kc == 0 else [])
                V(lambda e, d=xhc(kc): e.tensor_sub(d, d, bv(o_mrstdb, 512)))
            xh_ready = vch.now
            xb_war[bi] = [xh_ready, stats_done, sq_last]

            for mat in (("q",) if g < OWN_QB else ()) + ("k",):
                woff = o_wqT if mat == "q" else o_wkT
                for oc in range(8):
                    ri = oc % 2
                    dst = PA[:, ri * 512:(ri + 1) * 512]
                    for kc in range(8):
                        T(lambda e, lh=bv(woff + kc * D + oc * 128, 128), r=xhc(kc),
                          dst=dst, st=(kc == 0), sp=(kc == 7): e.matmul(
                            dst, lh, r, start=st, stop=sp),
                          waits=[xh_ready, qk_war[ri], prolog_gq] if kc == 0 else [],
                          inc=pe if kc == 7 else None)
                    grp = pe.now
                    if mat == "q":
                        qk_war[ri] = V(
                            lambda e, d=bv(o_qT + oc * TOK + g * 512, 512), s_=dst:
                            e.tensor_copy(d, s_), waits=[grp])
                    else:
                        kev = V(lambda e, s_=dst: e.tensor_copy(bv(o_ktmp, 512), s_),
                                waits=[grp, ktmp_war])
                        qk_war[ri] = kev
                        SY(lambda e, d=kT_hbm[oc * 128:(oc + 1) * 128, g * 512:(g + 1) * 512]:
                           e.dma_start(d, bv(o_ktmp, 512)), waits=[kev])
                        ktmp_war = sq.now

            for tt in range(4):
                c_glob = g * 4 + tt
                vr = tt % 2
                vps = PC if vr == 0 else PEx
                for kc in range(8):
                    for nb in range(2):
                        w = []
                        if kc == 0 and nb == 0:
                            w = [xh_ready, v_war[vr]]
                        T(lambda e, dst=vps[:, nb * 512:(nb + 1) * 512],
                          lh=xhc(kc)[:, tt * 128:(tt + 1) * 128],
                          r=bv(o_wvT + kc * D + nb * 512, 512),
                          st=(kc == 0), sp=(kc == 7): e.matmul(dst, lh, r, start=st, stop=sp),
                          waits=w, inc=pe if (kc == 7 and nb == 1) else None)
                vgrp = pe.now
                sbi = tt % 2
                vev = V(lambda e,
                        d=bv(o_vsb[sbi], 16 * 66).rearrange("p (s j) -> p s j", j=66)[:, :, 0:64],
                        s_=vps[:, :].rearrange("p (s j) -> p s j", j=64):
                        e.tensor_copy(d, s_),
                        waits=[vgrp, vsb_war[sbi], prolog_v])
                v_war[vr] = vev
                SY(lambda e, d=v_hbm.rearrange("r p c j -> p c r j")[:, c_glob, :, :],
                   s_=bv(o_vsb[sbi], 16 * 66).rearrange("p (r j) -> p r j", j=132):
                   e.dma_start(d, s_), waits=[vev])
                vsb_war[sbi] = sq.now
            xh_war[bi] = pe.now

        p1_pe = pe.now
        p1_v = vch.now
        p1_sq = sq.now

        if debug and stage <= 2:
            for oc in range(8):
                G(lambda e, d=dbg["q"][oc * 128:(oc + 1) * 128, :],
                  s=bv(o_qT + oc * TOK, TOK): e.dma_start(d, s), waits=[p1_v])
        if stage == 1:
            plan.op("sync", None, waits=[gq.now, sq.now])
            _emit(nc, es, plan)
            return nc

        PT_war = [[NONE, NONE], [NONE, NONE]]
        acc_war = p1_v
        scA_war = p1_v
        scB_war = pb_war
        bc2_war = p1_v
        stg_war = [NONE, NONE]
        kth_war = [NONE, NONE, NONE]
        vh_war = [NONE, NONE, NONE]
        kv_ready = {}
        o_bAf = o_rows

        def load_kv(p, hb, slot):
            kk = SY(lambda e, d=bv(o_kth[slot], 32 * 128),
                    s=kT_hbm[p * 128:(p + 1) * 128, hb * 4096:(hb + 1) * 4096]:
                    e.dma_start(d, s), waits=[p1_sq, p1_pe, kth_war[slot]])
            vv = SY(lambda e, d=bv(o_vh[slot], 32 * 132),
                    s=v_hbm[p, :, hb * 32:(hb + 1) * 32, :].rearrange("p c j -> p (c j)"):
                    e.dma_start(d, s), waits=[vh_war[slot]])
            kv_ready[(p, hb)] = (kk, vv)

        load_kv(0, 0, 0)
        load_kv(0, 1, 1)

        def emit_scores(p, g, s, extra):
            slot = ((2 * p) % 3) if s < 16 else ((2 * p + 1) % 3)
            kk = kv_ready[(p, s // 16)][0]
            cc0 = (s % 16) * 2
            res = []
            for (ps, war, base) in ((PA, scA_war, 0), (PB, scB_war, 64)):
                for ni, cc in enumerate((cc0, cc0 + 1)):
                    w = [kk, war] + extra if ni == 0 else []
                    T(lambda e, dst=ps[:, ni * 512:(ni + 1) * 512],
                      lh=bv(o_kth[slot] + cc * 128, 128)[base:base + 64, :],
                      r=bv(o_qT + p * TOK + g * 512, 512)[base:base + 64, :]:
                      e.matmul(dst, lh, r, start=True, stop=True),
                      waits=w, inc=pe if ni == 1 else None)
                res.append(pe.now)
            return res

        sAB = emit_scores(0, 0, 0, [p1_v])
        pv_p_end = NONE
        for p in range(NPAIR):
            for g in range(OWN_QB):
                for s in range(32):
                    slot = ((2 * p) % 3) if s < 16 else ((2 * p + 1) % 3)
                    vv = kv_ready[(p, s // 16)][1]
                    cc0 = (s % 16) * 2
                    ra = s % 2
                    expA = S(lambda e, d=bv(o_PTA[ra], 1024): e.activation(
                        d, PA[:, :], AF.Exp, scale=0.125),
                        waits=[sAB[0], PT_war[0][ra]])
                    scA_war = expA
                    expB = S(lambda e, d=bv(o_PTB[ra], 1024): e.activation(
                        d, PB[:, :], AF.Exp, scale=0.125),
                        waits=[sAB[1], PT_war[1][ra]])
                    scB_war = expB
                    if s + 1 < 32:
                        sAB = emit_scores(p, g, s + 1, [])
                    elif not (p == NPAIR - 1 and g == OWN_QB - 1):
                        np_, ng = (p, g + 1) if g + 1 < OWN_QB else (p + 1, 0)
                        sAB = emit_scores(np_, ng, 0, [])
                    first = (s == 0)
                    last = (s == 31)
                    for (hAB, pto, coloff, wexp) in ((0, o_PTA[ra], 0, expA),
                                                     (1, o_PTB[ra], 66, expB)):
                        for ni, cc in enumerate((cc0, cc0 + 1)):
                            w = []
                            if ni == 0:
                                w = [wexp, vv]
                                if first and hAB == 0:
                                    w.append(acc_war)
                            T(lambda e,
                              dst=PC[0:65, hAB * 512:hAB * 512 + 512],
                              lh=bv(o_vh[slot] + cc * 132 + coloff, 65),
                              r=bv(pto + ni * 512, 512),
                              st=(first and cc == 0 and ni == 0),
                              sp=(last and ni == 1):
                              e.matmul(dst, lh, r, start=st, stop=sp),
                              waits=w,
                              inc=pe if ni == 1 else None)
                        PT_war[hAB][ra] = pe.now
                pv_done = pe.now
                if g == 0 and p + 1 < NPAIR:
                    ns0 = (2 * p + 2) % 3
                    kth_war[ns0] = pv_p_end
                    vh_war[ns0] = pv_p_end
                    load_kv(p + 1, 0, ns0)
                if g == OWN_QB - 1:
                    pv_p_end = pv_done
                    if p + 1 < NPAIR:
                        ns1 = (2 * p + 3) % 3
                        kth_war[ns1] = pv_done
                        vh_war[ns1] = pv_done
                        load_kv(p + 1, 1, ns1)

                dcast = V(lambda e: e.tensor_copy(bv(o_recip, 1024)[64:65, :],
                                                  PC[64:65, :]), waits=[pv_done])
                if debug and stage == 2:
                    G(lambda e, d=dbg["l"][p][g:g + 1, :],
                      s=bv(o_recip, 1024)[64:65, :]: e.dma_start(d, s),
                      waits=[dcast])
                T(lambda e: e.matmul(PEx[0:64, 0:512], ONES[64:65, 0:64],
                                     bv(o_recip, 512)[64:65, :], start=True, stop=True),
                  waits=[dcast, bc2_war], inc=pe)
                T(lambda e: e.matmul(PEx[0:64, 512:1024], ONES[64:65, 0:64],
                                     bv(o_recip + 512, 512)[64:65, :], start=True, stop=True),
                  inc=pe)
                nbd = pe.now
                rcp = V(lambda e: e.reciprocal(af[0:64, o_bAf:o_bAf + 1024],
                                               PEx[0:64, :]), waits=[nbd])
                bc2_war = rcp
                V(lambda e, d=bv(o_O + p * TOK + g * 512, 512)[0:64, :]:
                  e.tensor_mul(d, PC[0:64, 0:512], af[0:64, o_bAf:o_bAf + 512]))
                sgi = g % 2
                mulB = V(lambda e, d=bv(o_stgB[sgi], 512)[0:64, :]:
                         e.tensor_mul(d, PC[0:64, 512:1024],
                                      af[0:64, o_bAf + 512:o_bAf + 1024]),
                         waits=[stg_war[sgi]])
                acc_war = vch.now
                SY(lambda e, d=bv(o_O + p * TOK + g * 512, 512)[64:128, :],
                   s_=bv(o_stgB[sgi], 512)[0:64, :]: e.dma_start(d, s_), waits=[mulB])
                stg_war[sgi] = sq.now

        p2_v = vch.now
        p2_sq = sq.now
        p2_pe = pe.now
        p2_s = sch.now

        if debug and stage == 2:
            for oc in range(8):
                G(lambda e, d=dbg["O"][oc * 128:(oc + 1) * 128, :],
                  s=bv(o_O + oc * TOK, TOK): e.dma_start(d, s), waits=[p2_v, p2_sq])
        if stage == 2:
            plan.op("sync", None, waits=[gq.now, sq.now])
            _emit(nc, es, plan)
            return nc

        proj_war = [p2_s, p2_s]
        fA_war = [NONE, NONE]
        fB_war = [NONE, NONE]
        fC_war = [NONE, NONE]
        pc_war = p2_v
        pex_war = bc2_war
        gate_free = NONE
        x2_writes = 0

        def fa(i):
            return af[:, o_fA + i * D:o_fA + (i + 1) * D]

        def fb(i):
            return af[:, o_fB + i * D:o_fB + (i + 1) * D]

        def fc(i):
            return af[:, o_fC + i * D:o_fC + (i + 1) * D]

        sq3_war = [NONE, NONE]
        st1 = [None] * 16

        def p3_s1(Ti):
            nonlocal x2_writes
            g, tt = Ti // 4, Ti % 4
            ri = Ti % 2
            ps = PA if ri == 0 else PB
            base = 6 * ri
            c_s, c_m, c_ss, c_var, c_sd, c_rstd = (
                cols[:, base + i:base + i + 1] for i in range(6))
            xown_ld = SY(lambda e, d=fa(ri), s=x_own[Ti * 128:(Ti + 1) * 128, :]:
                         e.dma_start(d, s), waits=[fA_war[ri]])
            for cc in range(8):
                for nb in range(2):
                    T(lambda e, dst=ps[:, nb * 512:(nb + 1) * 512],
                      lh=bv(o_O + cc * TOK + g * 512 + tt * 128, 128),
                      r=bv(o_woT + cc * D + nb * 512, 512),
                      st=(cc == 0), sp=(cc == 7): e.matmul(dst, lh, r, start=st, stop=sp),
                      waits=[p2_v, p2_sq, proj_war[ri], prolog_gq] if (cc == 0 and nb == 0) else [],
                      inc=pe if (cc == 7 and nb == 1) else None)
            prj = pe.now
            x2v = V(lambda e, d=fb(ri), s_=ps, a=fa(ri): e.scalar_tensor_tensor(
                d, s_[:, :], 1.0, a, ALU.mult, ALU.add),
                waits=[prj, xown_ld, fB_war[ri]])
            proj_war[ri] = x2v
            fA_war[ri] = x2v
            SY(lambda e, d=x2_hbm[Ti * 128:(Ti + 1) * 128, :], s_=fb(ri): e.dma_start(d, s_),
               waits=[x2v])
            x2_writes = sq.count
            fB_war[ri] = sq.now

            V(lambda e, s_=fb(ri): e.reduce_sum(c_s, s_, axis=AX.X), waits=[x2v])
            S(lambda e, s_=fb(ri), d=bv(o_sq3 + ri * 1024, 1024): e.activation(
                d, s_, AF.Square),
              waits=[x2v, sq3_war[ri]] + ([(pe, p2_pe[1])] if Ti < 2 else []))
            V(lambda e, s_=bv(o_sq3 + ri * 1024, 1024): e.reduce_sum(c_ss, s_, axis=AX.X),
              waits=[sch.now])
            sq3_war[ri] = vch.now
            V(lambda e: e.tensor_scalar(c_m, c_s, 1.0 / D, None, ALU.mult))
            V(lambda e: e.tensor_scalar(c_ss, c_ss, 1.0 / D, None, ALU.mult))
            V(lambda e: e.tensor_mul(c_var, c_m, c_m))
            V(lambda e: e.tensor_sub(c_var, c_ss, c_var))
            V(lambda e: e.tensor_scalar(c_var, c_var, float(EPS), None, ALU.add))
            sqev = S(lambda e: e.activation(c_sd, c_var, AF.Sqrt), waits=[vch.now])
            st1[Ti] = (ri, c_m, c_sd, c_rstd, sqev)

        def p3_s2(Ti):
            nonlocal gate_free, pc_war, pex_war
            ri, c_m, c_sd, c_rstd, sqev = st1[Ti]
            V(lambda e: e.reciprocal(c_rstd, c_sd), waits=[sqev])
            xh2tok = V(lambda e, d=fc(ri), s_=fb(ri): e.tensor_scalar(
                d, s_, c_m, c_rstd, ALU.subtract, ALU.mult), waits=[fC_war[ri]])
            for kc in range(8):
                w = [xh2tok]
                if Ti == 0 and kc < 2:
                    w.append(pc_war)
                if kc >= 2:
                    w.append(vch.now)
                T(lambda e, dst=PC[:, (kc % 2) * 512:(kc % 2) * 512 + 128],
                  s_=fc(ri)[:, kc * 128:(kc + 1) * 128]:
                  e.transpose(dst, s_, ident_f[:, :]), waits=w, inc=pe)
                trk = pe.now
                V(lambda e, d=bv(o_xh2T + kc * TOK + Ti * 128, 128),
                  s_=PC[:, (kc % 2) * 512:(kc % 2) * 512 + 128]: e.tensor_copy(d, s_),
                  waits=[trk])
            fC_war[ri] = vch.now
            xh2T_done = vch.now
            for kc in range(8):
                w = [xh2T_done] if kc == 0 else []
                if kc == 0:
                    w.append(gate_free)
                    if Ti == 0:
                        w.append(pex_war)
                T(lambda e, lh=bv(o_xh2T + kc * TOK + Ti * 128, 128),
                  r=bv(o_gwT + kc * 8, 8), st=(kc == 0), sp=(kc == 7):
                  e.matmul(PEx[:, 0:8], lh, r, start=st, stop=sp),
                  waits=w, inc=pe if kc == 7 else None)
            lg = pe.now
            cmb = comb_sb[:, Ti * 8:(Ti + 1) * 8]
            mcol = cols[:, 12:13]
            V(lambda e: e.reduce_max(mcol, PEx[:, 0:8], axis=AX.X, negate=True), waits=[lg])
            S(lambda e: e.activation(t8a[:, :], PEx[:, 0:8], AF.Exp, bias=mcol), waits=[vch.now])
            gate_free = sch.now
            V(lambda e: e.reduce_sum(cols[:, 13:14], t8a[:, :], axis=AX.X), waits=[sch.now])
            V(lambda e: e.reciprocal(cols[:, 14:15], cols[:, 13:14]))
            V(lambda e: e.tensor_scalar(t8a[:, :], t8a[:, :], cols[:, 14:15], None, ALU.mult))
            V(lambda e: e.memset(t8b[:, :], 0.0))
            for ei in range(8):
                V(lambda e, c=t8a[:, ei:ei + 1]: e.tensor_scalar(
                    t8c[:, :], t8a[:, :], c, None, ALU.is_gt))
                V(lambda e, d=t8b[:, ei:ei + 1]: e.reduce_sum(d, t8c[:, :], axis=AX.X))
                if ei > 0:
                    V(lambda e, c=t8a[:, ei:ei + 1], s_=t8a[:, 0:ei], d=t8c[:, 0:ei]:
                      e.tensor_scalar(d, s_, c, None, ALU.is_equal))
                    V(lambda e, s_=t8c[:, 0:ei]: e.reduce_sum(cols[:, 15:16], s_, axis=AX.X))
                    V(lambda e, d=t8b[:, ei:ei + 1]: e.tensor_add(d, d, cols[:, 15:16]))
            V(lambda e: e.tensor_scalar(t8c[:, :], t8b[:, :], 2.0, None, ALU.is_lt))
            V(lambda e, d=cmb: e.tensor_mul(d, t8a[:, :], t8c[:, :]))

        p3_s1(0)
        for Ti in range(16):
            if Ti + 1 < 16:
                p3_s1(Ti + 1)
            p3_s2(Ti)

        p3_v = vch.now
        p3_sq = sq.now
        p3_pe = pe.now
        p3_s = sch.now

        if debug and stage == 3:
            for kc in range(8):
                G(lambda e, d=dbg["xh2T"][kc * 128:(kc + 1) * 128, :],
                  s=bv(o_xh2T + kc * TOK, TOK): e.dma_start(d, s), waits=[p3_v])
            G(lambda e, d=dbg["comb"].rearrange("t p e -> p t e"),
              s=comb_sb[:, :].rearrange("p (t e) -> p t e", e=8): e.dma_start(d, s),
              waits=[p3_v])
        if stage == 3:
            plan.op("sync", None, waits=[gq.now, sq.now])
            _emit(nc, es, plan)
            return nc

        w1c_war = [p3_pe, p3_pe]
        w3c_war = [p3_pe, p3_pe]
        w2h_war = [p3_pe, p3_pe]
        st_war = [p3_pe, p3_pe]
        h1_war = [p3_v, p3_v]
        h3_war = [p3_v, p3_v]
        hhat_war = [p3_pe] * EHC
        yps_war = [p3_v, p3_s]
        yacc_war = [NONE] * 8

        w1v = w1.rearrange("e (c p) j -> e p c j", p=128)
        w3v = w3.rearrange("e (c p) j -> e p c j", p=128)
        sw1v = sw1.rearrange("(c p) j -> p c j", p=128)
        sw3v = sw3.rearrange("(c p) j -> p c j", p=128)

        first_ffn = [True]

        def ffn_branch(pr, njc, w1s, w3s, w2s, comb_fn):
            nonlocal w1c_war, w3c_war, w2h_war, st_war, h1_war, h3_war, hhat_war, yps_war, yacc_war
            tq = pr * 1024
            w1ld = w3ld = None
            for jc in range(njc):
                if jc % 2 == 0:
                    wb = (jc // 2) % 2
                    n2 = min(2, njc - jc) * 128
                    w1ld = G(lambda e,
                             d=bv(o_w1c[wb], 8 * n2).rearrange("p (c j) -> p c j", c=8),
                             s=w1s[:, :, jc * 128:jc * 128 + n2]: e.dma_start(d, s),
                             waits=[w1c_war[wb]])
                    w3ld = G(lambda e,
                             d=bv(o_w3c[wb], 8 * n2).rearrange("p (c j) -> p c j", c=8),
                             s=w3s[:, :, jc * 128:jc * 128 + n2]: e.dma_start(d, s),
                             waits=[w3c_war[wb]])
                wb = (jc // 2) % 2
                n2 = min(2, njc - (jc // 2) * 2) * 128
                pair_last = (jc % 2 == 1) or (jc == njc - 1)
                for th in range(2):
                    it = jc * 2 + th
                    hb = it % 2
                    h1ps = PA[:, hb * 512:(hb + 1) * 512]
                    h3ps = PB[:, hb * 512:(hb + 1) * 512]
                    for kc in range(8):
                        w = []
                        if kc == 0:
                            w = [w1ld, h1_war[hb]]
                            if first_ffn[0]:
                                w.append(p3_v)
                        T(lambda e, dst=h1ps,
                          lh=bv(o_w1c[wb] + kc * n2 + (jc % 2) * 128, 128),
                          r=bv(o_xh2T + kc * TOK + tq + th * 512, 512),
                          st=(kc == 0), sp=(kc == 7):
                          e.matmul(dst, lh, r, start=st, stop=sp),
                          waits=w, inc=pe if kc == 7 else None)
                    first_ffn[0] = False
                    h1g = pe.now
                    if pair_last and th == 1:
                        w1c_war[wb] = h1g
                    for kc in range(8):
                        T(lambda e, dst=h3ps,
                          lh=bv(o_w3c[wb] + kc * n2 + (jc % 2) * 128, 128),
                          r=bv(o_xh2T + kc * TOK + tq + th * 512, 512),
                          st=(kc == 0), sp=(kc == 7):
                          e.matmul(dst, lh, r, start=st, stop=sp),
                          waits=[w3ld, h3_war[hb]] if kc == 0 else [],
                          inc=pe if kc == 7 else None)
                    h3g = pe.now
                    if pair_last and th == 1:
                        w3c_war[wb] = h3g
                    sb = it % 2
                    sdone = S(lambda e, d=bv(o_stile[sb], 512), s_=h1ps: e.activation(
                        d, s_, AF.Silu), waits=[h1g, st_war[sb]])
                    h1_war[hb] = sdone
                    hev = V(lambda e, d=bv(o_hh + jc * 1024 + th * 512, 512), s_=h3ps,
                            t_=bv(o_stile[sb], 512):
                            e.tensor_mul(d, s_, t_),
                            waits=[h3g, sdone, hhat_war[jc]])
                    h3_war[hb] = hev
                    st_war[sb] = hev
            hh_all = vch.now
            per = (njc + 1) // 2
            for jb in range(2):
                lo = jb * per
                hi = min(njc, lo + per)
                mid = lo + (hi - lo) // 2
                slab = bv(o_w2h[jb], (hi - lo) * D).rearrange("p (c j) -> p c j",
                                                              c=hi - lo)
                w2a = G(lambda e, d=slab[:, 0:mid - lo, :],
                        s=w2s[lo * 128:mid * 128, :].rearrange("(c p) j -> p c j", p=128):
                        e.dma_start(d, s), waits=[w2h_war[jb]])
                w2b = G(lambda e, d=slab[:, mid - lo:hi - lo, :],
                        s=w2s[mid * 128:hi * 128, :].rearrange("(c p) j -> p c j", p=128):
                        e.dma_start(d, s), waits=[])
                for tt in range(8):
                    Ti = pr * 8 + tt
                    yr = tt % 2
                    yps = PC if yr == 0 else PEx
                    for jj in range(hi - lo):
                        jc = lo + jj
                        for nb in range(2):
                            T(lambda e, dst=yps[:, nb * 512:(nb + 1) * 512],
                              lh=bv(o_hh + jc * 1024 + (tt // 4) * 512 + (tt % 4) * 128, 128),
                              r=bv(o_w2h[jb] + jj * D + nb * 512, 512),
                              st=(jj == 0), sp=(jj == hi - lo - 1):
                              e.matmul(dst, lh, r, start=st, stop=sp),
                              waits=[w2a, w2b, hh_all, yps_war[yr]] if (jj == 0 and nb == 0) else [],
                              inc=pe if (jj == hi - lo - 1 and nb == 1) else None)
                    ygrp = pe.now
                    yps_war[yr] = V(lambda e,
                                    d=af[:, o_yacc + tt * D:o_yacc + (tt + 1) * D],
                                    s_=yps, c=comb_fn(Ti):
                                    e.scalar_tensor_tensor(d, s_[:, :], c, d, ALU.mult, ALU.add),
                                    waits=[ygrp, yacc_war[tt]])
                    yacc_war[tt] = yps_war[yr]
                w2h_war[jb] = pe.now
            for jc in range(njc):
                hhat_war[jc] = pe.now

        def dump(dst, src_off, n):
            dv = G(lambda e, d=dst, s=bv(src_off, n): e.dma_start(d, s),
                   waits=[vch.now])
            plan.op("vector", None, waits=[dv])
            plan.op("tensor", None, waits=[dv])

        def dumpf(dst, src_off, n):
            dv = G(lambda e, d=dst, s=af[:, src_off:src_off + n]: e.dma_start(d, s),
                   waits=[vch.now])
            plan.op("vector", None, waits=[dv])
            plan.op("tensor", None, waits=[dv])

        out_dmas = 0
        f4 = [NONE, NONE]
        for pr in range(2):
            for tt in range(8):
                w = [yacc_war[tt]]
                if pr == 0:
                    w.append(p3_pe)
                yacc_war[tt] = V(lambda e, d=af[:, o_yacc + tt * D:o_yacc + (tt + 1) * D]:
                                 e.memset(d, 0.0), waits=w)
            for ei in range(EXPERTS):
                ffn_branch(pr, EHC, w1v[ei], w3v[ei], w2[ei],
                           (lambda ei=ei: (lambda Ti: comb_sb[:, Ti * 8 + ei:Ti * 8 + ei + 1]))())
                if debug and stage == 4 and pr == 0 and ei == 0:
                    dump(dbg["hhat"], o_hh, EHC * 1024)
                    dumpf(dbg["yaccE"], o_yacc, 8 * D)
            ffn_branch(pr, SHC, sw1v, sw3v, sw2, lambda Ti: 1.0)
            if debug and stage == 4 and pr == 0:
                dumpf(dbg["yaccS"], o_yacc, 8 * D)
            for tt in range(8):
                Ti = pr * 8 + tt
                ri = tt % 2
                x2ld = SY(lambda e, d=fa(ri), s=x2_hbm[Ti * 128:(Ti + 1) * 128, :]:
                          e.dma_start(d, s), waits=[(sq, x2_writes), fA_war[ri]])
                addv = V(lambda e, d=fb(ri), a=af[:, o_yacc + tt * D:o_yacc + (tt + 1) * D],
                         b=fa(ri): e.tensor_add(d, a, b),
                         waits=[x2ld, yacc_war[tt], fB_war[ri], f4[ri]])
                fA_war[ri] = addv
                yacc_war[tt] = addv
                SY(lambda e, d=out[Ti * 128:(Ti + 1) * 128, :], s_=fb(ri): e.dma_start(d, s_),
                   waits=[addv])
                fB_war[ri] = sq.now
                f4[ri] = sq.now

        plan.op("sync", None, waits=[sq.now, gq.now])
        _emit(nc, es, plan)
    return nc


def _emit(nc, es, plan):
    for s in plan.streams:
        s.sem = es.enter_context(nc.semaphore(s.name))
    block = es.enter_context(nc.Block())

    @block.tensor
    def _(t):
        plan.replay("tensor", t)

    @block.vector
    def _(v):
        plan.replay("vector", v)

    @block.scalar
    def _(s):
        plan.replay("scalar", s)

    @block.gpsimd
    def _(g):
        plan.replay("gpsimd", g)

    @block.sync
    def _(sy):
        plan.replay("sync", sy)


_ONES = np.ones((128, 128), dtype=ml_dtypes.bfloat16)
_IDENTF = np.eye(128, dtype=np.float32)


def make_in_maps(inputs):
    x = np.asarray(inputs["L0_attn_x"])
    base = {
        "wqT": np.ascontiguousarray(np.asarray(inputs["L0_attn_w_q"]).T),
        "wkT": np.ascontiguousarray(np.asarray(inputs["L0_attn_w_k"]).T),
        "wvT": np.ascontiguousarray(np.asarray(inputs["L0_attn_w_v"]).T),
        "woT": np.ascontiguousarray(np.asarray(inputs["L0_attn_w_o"]).T),
        "gwT": np.ascontiguousarray(np.asarray(inputs["L0_ffn_gate_w"]).T),
        "w1": np.ascontiguousarray(np.asarray(inputs["L0_ffn_w1"])),
        "w2": np.ascontiguousarray(np.asarray(inputs["L0_ffn_w2"])),
        "w3": np.ascontiguousarray(np.asarray(inputs["L0_ffn_w3"])),
        "sw1": np.ascontiguousarray(np.asarray(inputs["L0_ffn_sw1"])),
        "sw2": np.ascontiguousarray(np.asarray(inputs["L0_ffn_sw2"])),
        "sw3": np.ascontiguousarray(np.asarray(inputs["L0_ffn_sw3"])),
        "ones_in": _ONES,
        "identf_in": _IDENTF,
    }
    in_maps = []
    for c in range(8):
        b, m = c // 4, c % 4
        xr = np.roll(x[b], -TOK * m, axis=0)
        im = dict(base)
        im["xT"] = np.ascontiguousarray(xr.T)
        im["x_own"] = np.ascontiguousarray(xr[0:TOK])
        in_maps.append(im)
    return in_maps


def kernel(**inputs):
    nc = build_nc(stage=4, debug=False)
    in_maps = make_in_maps(inputs)
    res = run_bass_kernel_spmd(nc, in_maps, core_ids=list(range(8)))
    out = np.empty((2, SEQ, D), dtype=np.float32)
    for c in range(8):
        b, m = c // 4, c % 4
        out[b, m * TOK:(m + 1) * TOK, :] = res.results[c]["out"]
    return out


# version 12
# speedup vs baseline: 1.0257x; 1.0257x over previous
import contextlib
import numpy as np
import ml_dtypes
import concourse.bass as bass
import concourse.mybir as mybir
from concourse.bass_utils import run_bass_kernel_spmd

F32 = mybir.dt.float32
BF16 = mybir.dt.bfloat16
AF = mybir.ActivationFunctionType
ALU = mybir.AluOpType
AX = mybir.AxisListType

D = 1024
SEQ = 8192
TOK = 2048
NQB = 16
OWN_QB = 4
NPAIR = 8
EXPERTS = 8
EH = 2816
EHC = 22
SHH = 1536
SHC = 12
NQTR = 4
EPS = 1e-5

ENGS = ("tensor", "vector", "scalar", "gpsimd", "sync")
P3_SUB = 5
NONE = (None, 0)


class Stream:
    __slots__ = ("name", "step", "count", "sem")

    def __init__(self, name, step=1):
        self.name, self.step, self.count = name, step, 0
        self.sem = None

    @property
    def now(self):
        return (self, self.count)


class Plan:
    def __init__(self):
        self.ops = {e: [] for e in ENGS}
        self.streams = []

    def stream(self, name, step=1):
        s = Stream(name, step)
        self.streams.append(s)
        return s

    def op(self, eng, fn, waits=(), inc=None):
        ws = []
        for w in waits:
            if w is None:
                continue
            s, c = w
            if s is not None and c > 0:
                ws.append((s, c))
        if inc is not None:
            inc.count += inc.step
        self.ops[eng].append((fn, ws, inc))

    def replay(self, ename, eng):
        waited = {}
        for fn, ws, inc in self.ops[ename]:
            for s, c in ws:
                if waited.get(s.name, 0) < c:
                    eng.wait_ge(s.sem, c)
                    waited[s.name] = c
            if fn is None:
                continue
            inst = fn(eng)
            if inc is not None:
                inst.then_inc(inc.sem, inc.step)


def build_nc(stage=4, debug=False):
    nc = bass.Bass("TRN2", target_bir_lowering=False)

    xT = nc.dram_tensor("xT", [D, SEQ], F32, kind="ExternalInput").ap()
    x_own = nc.dram_tensor("x_own", [TOK, D], F32, kind="ExternalInput").ap()
    wqT = nc.dram_tensor("wqT", [D, D], F32, kind="ExternalInput").ap()
    wkT = nc.dram_tensor("wkT", [D, D], F32, kind="ExternalInput").ap()
    wvT = nc.dram_tensor("wvT", [D, D], F32, kind="ExternalInput").ap()
    woT = nc.dram_tensor("woT", [D, D], F32, kind="ExternalInput").ap()
    gwT = nc.dram_tensor("gwT", [D, EXPERTS], F32, kind="ExternalInput").ap()
    w1 = nc.dram_tensor("w1", [EXPERTS, D, EH], F32, kind="ExternalInput").ap()
    w2 = nc.dram_tensor("w2", [EXPERTS, EH, D], F32, kind="ExternalInput").ap()
    w3 = nc.dram_tensor("w3", [EXPERTS, D, EH], F32, kind="ExternalInput").ap()
    sw1 = nc.dram_tensor("sw1", [D, SHH], F32, kind="ExternalInput").ap()
    sw2 = nc.dram_tensor("sw2", [SHH, D], F32, kind="ExternalInput").ap()
    sw3 = nc.dram_tensor("sw3", [D, SHH], F32, kind="ExternalInput").ap()
    ones_in = nc.dram_tensor("ones_in", [128, 128], BF16, kind="ExternalInput").ap()
    identf_in = nc.dram_tensor("identf_in", [128, 128], F32, kind="ExternalInput").ap()

    scratch_kind = "ExternalOutput" if (debug and stage <= 2) else "Internal"
    kT_hbm = nc.dram_tensor("kT_hbm", [D, SEQ], BF16, kind=scratch_kind).ap()
    v_hbm = nc.dram_tensor("v_hbm", [NPAIR, 128, SEQ // 128, 132], BF16,
                           kind=scratch_kind).ap()
    x2_kind = "ExternalOutput" if (debug and stage == 3) else "Internal"
    x2_hbm = nc.dram_tensor("x2_hbm", [TOK, D], F32, kind=x2_kind).ap()

    out = None
    if stage >= 4:
        out = nc.dram_tensor("out", [TOK, D], F32, kind="ExternalOutput").ap()
    dbg = {}
    if debug:
        if stage <= 2:
            dbg["q"] = nc.dram_tensor("dbg_q", [D, TOK], F32, kind="ExternalOutput").ap()
        if stage == 2:
            dbg["O"] = nc.dram_tensor("dbg_O", [D, TOK], F32, kind="ExternalOutput").ap()
            dbg["l"] = nc.dram_tensor("dbg_l", [NPAIR, OWN_QB, 1024], F32, kind="ExternalOutput").ap()
        if stage == 3:
            dbg["xh2T"] = nc.dram_tensor("dbg_xh2T", [D, TOK], F32, kind="ExternalOutput").ap()
            dbg["comb"] = nc.dram_tensor("dbg_comb", [TOK // 128, 128, EXPERTS], F32,
                                         kind="ExternalOutput").ap()
        if stage == 4:
            dbg["hhat"] = nc.dram_tensor("dbg_hhat", [128, EHC * 1024], BF16,
                                         kind="ExternalOutput").ap()
            dbg["yaccE"] = nc.dram_tensor("dbg_yaccE", [128, 8 * D], F32,
                                          kind="ExternalOutput").ap()
            dbg["yaccS"] = nc.dram_tensor("dbg_yaccS", [128, 8 * D], F32,
                                          kind="ExternalOutput").ap()

    plan = Plan()
    pe = plan.stream("pe")
    vch = plan.stream("vch")
    sch = plan.stream("sch")
    gq = plan.stream("gq", 16)
    sq = plan.stream("sq", 16)

    def T(fn, waits=(), inc=None):
        plan.op("tensor", fn, waits, inc)

    def V(fn, waits=()):
        plan.op("vector", fn, list(waits) + [vch.now], inc=vch)
        return vch.now

    def S(fn, waits=()):
        plan.op("scalar", fn, list(waits) + [sch.now], inc=sch)
        return sch.now

    def G(fn, waits=()):
        plan.op("gpsimd", fn, waits, inc=gq)
        return gq.now

    def SY(fn, waits=()):
        plan.op("sync", fn, waits, inc=sq)
        return sq.now

    es = contextlib.ExitStack()
    with es:
        NB16 = 73408
        NF32 = 16000
        ab = es.enter_context(nc.sbuf_tensor("AB", [128, NB16], BF16))
        af = es.enter_context(nc.sbuf_tensor("AF", [128, NF32], F32))

        off = [0]

        def alloc(n):
            o = (off[0] + 1) // 2 * 2
            off[0] = o + n
            assert off[0] <= NB16
            return o

        o_ones = alloc(128)
        o_ident = alloc(0)
        o_woT = alloc(8 * D)
        o_gwT = alloc(8 * 8)
        o_qT = alloc(8 * TOK)
        o_xh2T = alloc(8 * TOK)
        o_O = alloc(8 * TOK)

        p1o = [o_xh2T]

        def alloc_p1(n):
            o = (p1o[0] + 1) // 2 * 2
            p1o[0] = o + n
            assert p1o[0] <= NB16, p1o[0]
            return o

        o_wqT = alloc_p1(8 * D)
        o_wkT = alloc_p1(8 * D)
        o_wvT = alloc_p1(8 * D)
        o_xb = [alloc_p1(8 * 512) for _ in range(2)]
        o_xh = [alloc_p1(8 * 512) for _ in range(2)]
        o_sq1 = alloc_p1(4 * 512)
        o_ktmp = alloc_p1(512)
        o_vsb = [alloc_p1(16 * 66) for _ in range(2)]
        o_rstdb = alloc_p1(512)
        o_mrstdb = alloc_p1(512)
        o_rowsbf = alloc_p1(2 * 512)

        p2a = [o_xh2T]

        def alloc_p2a(n):
            o = (p2a[0] + 1) // 2 * 2
            p2a[0] = o + n
            assert p2a[0] <= o_O, p2a[0]
            return o

        o_kth = [alloc_p2a(32 * 128) for _ in range(3)]
        o_PTA = [alloc_p2a(1024) for _ in range(2)]
        o_PTB = [alloc_p2a(1024) for _ in range(2)]
        p2b = [o_O + 8 * TOK]

        def alloc_p2b(n):
            o = (p2b[0] + 1) // 2 * 2
            p2b[0] = o + n
            assert p2b[0] <= NB16, p2b[0]
            return o

        o_vh = [alloc_p2b(32 * 132) for _ in range(3)]
        o_recip = alloc_p2b(1024)
        o_bA = alloc_p2b(512)
        o_bB = alloc_p2b(512)
        o_stgB = [alloc_p2b(512) for _ in range(2)]

        o_sq3 = o_vh[0]

        o_hh = (o_woT + 1) // 2 * 2
        assert o_hh + EHC * 1024 <= o_xh2T, (o_hh, o_xh2T)
        p4b = [o_O]

        def alloc_p4b(n):
            o = (p4b[0] + 1) // 2 * 2
            p4b[0] = o + n
            assert p4b[0] <= NB16, p4b[0]
            return o

        o_w1c = [alloc_p4b(8 * 256) for _ in range(2)]
        o_w3c = [alloc_p4b(8 * 256) for _ in range(2)]
        o_w2h = [alloc_p4b(11 * D) for _ in range(2)]
        o_stile = [alloc_p4b(512) for _ in range(2)]

        fo = [0]

        def falloc(n):
            o = fo[0]
            fo[0] = o + n
            assert fo[0] <= NF32
            return o

        o_yacc = falloc(8 * D)
        o_fC = o_yacc
        o_rows = o_yacc + 2 * D
        o_lrow = o_yacc + 5 * D
        o_fA = falloc(2 * D)
        o_fB = falloc(2 * D)

        ident_f = es.enter_context(nc.sbuf_tensor("identf", [128, 128], F32))
        cols = es.enter_context(nc.sbuf_tensor("cols", [128, 16], F32))
        t8a = es.enter_context(nc.sbuf_tensor("t8a", [128, 8], F32))
        t8b = es.enter_context(nc.sbuf_tensor("t8b", [128, 8], F32))
        t8c = es.enter_context(nc.sbuf_tensor("t8c", [128, 8], F32))
        comb_sb = es.enter_context(nc.sbuf_tensor("comb", [128, 16 * 8], F32))

        PA = es.enter_context(nc.psum_tensor("PA", [128, 1024], F32))
        PB = es.enter_context(nc.psum_tensor("PB", [128, 1024], F32))
        PC = es.enter_context(nc.psum_tensor("PC", [128, 1024], F32))
        PEx = es.enter_context(nc.psum_tensor("PEx", [128, 1024], F32))

        def bv(o, n):
            return ab[:, o:o + n]

        ONES = bv(o_ones, 128)

        G(lambda e, d=ONES, s=ones_in: e.dma_start(d, s))
        G(lambda e, d=ident_f[:, :], s=identf_in: e.dma_start(d, s))
        for (o, src, n) in ((o_woT, woT, D), (o_gwT, gwT, 8),
                            (o_wqT, wqT, D), (o_wkT, wkT, D), (o_wvT, wvT, D)):
            d = bv(o, 8 * n).rearrange("p (c n) -> p c n", c=8)
            s = src.rearrange("(c p) n -> p c n", p=128)
            G(lambda e, d=d, s=s: e.dma_start(d, s))
        prolog_gq = gq.now
        for b in range(2):
            d = bv(o_vsb[b], 16 * 66).rearrange("p (s j) -> p s j", j=66)[:, :, 64:66]
            V(lambda e, d=d: e.memset(d, 1.0))
        prolog_v = vch.now

        xb_war = [NONE, NONE]
        xh_war = [NONE, NONE]
        ktmp_war = NONE
        vsb_war = [NONE, NONE]
        qk_war = [NONE, NONE]
        v_war = [NONE, NONE]
        pb_war = NONE

        xb_war = [[], []]
        for g in range(NQB):
            bi = g % 2
            xbv = bv(o_xb[bi], 8 * 512).rearrange("p (c t) -> p c t", c=8)
            src = xT.rearrange("(c p) t -> p c t", p=128)[:, :, g * 512:(g + 1) * 512]
            ld = G(lambda e, d=xbv, s=src: e.dma_start(d, s), waits=xb_war[bi])

            def xbc(kc, bi=bi):
                return bv(o_xb[bi] + kc * 512, 512)

            def xhc(kc, bi=bi):
                return bv(o_xh[bi] + kc * 512, 512)

            for kc in range(8):
                T(lambda e, r=xbc(kc), st=(kc == 0), sp=(kc == 7): e.matmul(
                    PB[0:1, 0:512], ONES[:, 0:1], r, start=st, stop=sp),
                  waits=[ld, pb_war] if kc == 0 else [],
                  inc=pe if kc == 7 else None)
            sq_last = None
            for half in range(2):
                sw = [ld] + ([pe.now] if half == 1 else [])
                for j in range(4):
                    kc = half * 4 + j
                    S(lambda e, d=bv(o_sq1 + j * 512, 512), s_=xbc(kc): e.activation(
                        d, s_, AF.Square), waits=sw if j == 0 else [])
                sqr = sch.now
                sq_last = sqr
                for j in range(4):
                    kc = half * 4 + j
                    T(lambda e, r=bv(o_sq1 + j * 512, 512), st=(kc == 0), sp=(kc == 7): e.matmul(
                        PB[0:1, 512:1024], ONES[:, 0:1], r, start=st, stop=sp),
                      waits=[sqr] if j == 0 else [],
                      inc=pe if kc == 7 else None)
            stats_done = pe.now

            A1, A2, A3, A4 = (af[:, o_rows + i * 512:o_rows + (i + 1) * 512]
                              for i in range(4))
            V(lambda e: e.tensor_copy(ab[0:1, o_rowsbf:o_rowsbf + 512],
                                      PB[0:1, 0:512]), waits=[stats_done])
            rows_ready = V(lambda e: e.tensor_copy(
                ab[0:1, o_rowsbf + 512:o_rowsbf + 1024], PB[0:1, 512:1024]))
            T(lambda e: e.matmul(PB[:, 0:512], ONES[0:1, :], ab[0:1, o_rowsbf:o_rowsbf + 512],
                                 start=True, stop=True),
              waits=[rows_ready, pb_war], inc=pe)
            T(lambda e: e.matmul(PB[:, 512:1024], ONES[0:1, :], ab[0:1, o_rowsbf + 512:o_rowsbf + 1024],
                                 start=True, stop=True), inc=pe)
            bc_done = pe.now
            V(lambda e: e.tensor_scalar(A1, PB[:, 0:512], 1.0 / D, None, ALU.mult),
              waits=[bc_done])
            V(lambda e: e.tensor_scalar(A2, PB[:, 512:1024], 1.0 / D, None, ALU.mult))
            pb_war = vch.now
            V(lambda e: e.tensor_mul(A3, A1, A1))
            V(lambda e: e.tensor_sub(A2, A2, A3))
            V(lambda e: e.tensor_scalar(A2, A2, float(EPS), None, ALU.add))
            S(lambda e: e.activation(A3, A2, AF.Sqrt), waits=[vch.now])
            V(lambda e: e.reciprocal(A4, A3), waits=[sch.now])
            V(lambda e: e.tensor_copy(bv(o_rstdb, 512), A4))
            V(lambda e, d=bv(o_mrstdb, 512): e.tensor_mul(d, A1, A4))

            for kc in range(8):
                V(lambda e, d=xhc(kc), s_=xbc(kc): e.tensor_mul(d, s_, bv(o_rstdb, 512)),
                  waits=[xh_war[bi]] if kc == 0 else [])
                V(lambda e, d=xhc(kc): e.tensor_sub(d, d, bv(o_mrstdb, 512)))
            xh_ready = vch.now
            xb_war[bi] = [xh_ready, stats_done, sq_last]

            for mat in (("q",) if g < OWN_QB else ()) + ("k",):
                woff = o_wqT if mat == "q" else o_wkT
                for oc in range(8):
                    ri = oc % 2
                    dst = PA[:, ri * 512:(ri + 1) * 512]
                    for kc in range(8):
                        T(lambda e, lh=bv(woff + kc * D + oc * 128, 128), r=xhc(kc),
                          dst=dst, st=(kc == 0), sp=(kc == 7): e.matmul(
                            dst, lh, r, start=st, stop=sp),
                          waits=[xh_ready, qk_war[ri], prolog_gq] if kc == 0 else [],
                          inc=pe if kc == 7 else None)
                    grp = pe.now
                    if mat == "q":
                        qk_war[ri] = V(
                            lambda e, d=bv(o_qT + oc * TOK + g * 512, 512), s_=dst:
                            e.tensor_copy(d, s_), waits=[grp])
                    else:
                        kev = V(lambda e, s_=dst: e.tensor_copy(bv(o_ktmp, 512), s_),
                                waits=[grp, ktmp_war])
                        qk_war[ri] = kev
                        SY(lambda e, d=kT_hbm[oc * 128:(oc + 1) * 128, g * 512:(g + 1) * 512]:
                           e.dma_start(d, bv(o_ktmp, 512)), waits=[kev])
                        ktmp_war = sq.now

            for tt in range(4):
                c_glob = g * 4 + tt
                vr = tt % 2
                vps = PC if vr == 0 else PEx
                for kc in range(8):
                    for nb in range(2):
                        w = []
                        if kc == 0 and nb == 0:
                            w = [xh_ready, v_war[vr]]
                        T(lambda e, dst=vps[:, nb * 512:(nb + 1) * 512],
                          lh=xhc(kc)[:, tt * 128:(tt + 1) * 128],
                          r=bv(o_wvT + kc * D + nb * 512, 512),
                          st=(kc == 0), sp=(kc == 7): e.matmul(dst, lh, r, start=st, stop=sp),
                          waits=w, inc=pe if (kc == 7 and nb == 1) else None)
                vgrp = pe.now
                sbi = tt % 2
                vev = V(lambda e,
                        d=bv(o_vsb[sbi], 16 * 66).rearrange("p (s j) -> p s j", j=66)[:, :, 0:64],
                        s_=vps[:, :].rearrange("p (s j) -> p s j", j=64):
                        e.tensor_copy(d, s_),
                        waits=[vgrp, vsb_war[sbi], prolog_v])
                v_war[vr] = vev
                SY(lambda e, d=v_hbm.rearrange("r p c j -> p c r j")[:, c_glob, :, :],
                   s_=bv(o_vsb[sbi], 16 * 66).rearrange("p (r j) -> p r j", j=132):
                   e.dma_start(d, s_), waits=[vev])
                vsb_war[sbi] = sq.now
            xh_war[bi] = pe.now

        p1_pe = pe.now
        p1_v = vch.now
        p1_sq = sq.now

        if debug and stage <= 2:
            for oc in range(8):
                G(lambda e, d=dbg["q"][oc * 128:(oc + 1) * 128, :],
                  s=bv(o_qT + oc * TOK, TOK): e.dma_start(d, s), waits=[p1_v])
        if stage == 1:
            plan.op("sync", None, waits=[gq.now, sq.now])
            _emit(nc, es, plan)
            return nc

        PT_war = [[NONE, NONE], [NONE, NONE]]
        acc_war = p1_v
        scA_war = p1_v
        scB_war = pb_war
        bc2_war = p1_v
        stg_war = [NONE, NONE]
        kth_war = [NONE, NONE, NONE]
        vh_war = [NONE, NONE, NONE]
        kv_ready = {}
        o_bAf = o_rows

        def load_kv(p, hb, slot):
            kk = SY(lambda e, d=bv(o_kth[slot], 32 * 128),
                    s=kT_hbm[p * 128:(p + 1) * 128, hb * 4096:(hb + 1) * 4096]:
                    e.dma_start(d, s), waits=[p1_sq, p1_pe, kth_war[slot]])
            vv = SY(lambda e, d=bv(o_vh[slot], 32 * 132),
                    s=v_hbm[p, :, hb * 32:(hb + 1) * 32, :].rearrange("p c j -> p (c j)"):
                    e.dma_start(d, s), waits=[vh_war[slot]])
            kv_ready[(p, hb)] = (kk, vv)

        load_kv(0, 0, 0)
        load_kv(0, 1, 1)

        def emit_scores(p, g, s, extra):
            slot = ((2 * p) % 3) if s < 16 else ((2 * p + 1) % 3)
            kk = kv_ready[(p, s // 16)][0]
            cc0 = (s % 16) * 2
            res = []
            for (ps, war, base) in ((PA, scA_war, 0), (PB, scB_war, 64)):
                for ni, cc in enumerate((cc0, cc0 + 1)):
                    w = [kk, war] + extra if ni == 0 else []
                    T(lambda e, dst=ps[:, ni * 512:(ni + 1) * 512],
                      lh=bv(o_kth[slot] + cc * 128, 128)[base:base + 64, :],
                      r=bv(o_qT + p * TOK + g * 512, 512)[base:base + 64, :]:
                      e.matmul(dst, lh, r, start=True, stop=True),
                      waits=w, inc=pe if ni == 1 else None)
                res.append(pe.now)
            return res

        sAB = emit_scores(0, 0, 0, [p1_v])
        pv_p_end = NONE
        for p in range(NPAIR):
            for g in range(OWN_QB):
                for s in range(32):
                    slot = ((2 * p) % 3) if s < 16 else ((2 * p + 1) % 3)
                    vv = kv_ready[(p, s // 16)][1]
                    cc0 = (s % 16) * 2
                    ra = s % 2
                    expA = S(lambda e, d=bv(o_PTA[ra], 1024): e.activation(
                        d, PA[:, :], AF.Exp, scale=0.125),
                        waits=[sAB[0], PT_war[0][ra]])
                    scA_war = expA
                    expB = S(lambda e, d=bv(o_PTB[ra], 1024): e.activation(
                        d, PB[:, :], AF.Exp, scale=0.125),
                        waits=[sAB[1], PT_war[1][ra]])
                    scB_war = expB
                    if s + 1 < 32:
                        sAB = emit_scores(p, g, s + 1, [])
                    elif not (p == NPAIR - 1 and g == OWN_QB - 1):
                        np_, ng = (p, g + 1) if g + 1 < OWN_QB else (p + 1, 0)
                        sAB = emit_scores(np_, ng, 0, [])
                    first = (s == 0)
                    last = (s == 31)
                    for (hAB, pto, coloff, wexp) in ((0, o_PTA[ra], 0, expA),
                                                     (1, o_PTB[ra], 66, expB)):
                        for ni, cc in enumerate((cc0, cc0 + 1)):
                            w = []
                            if ni == 0:
                                w = [wexp, vv]
                                if first and hAB == 0:
                                    w.append(acc_war)
                            T(lambda e,
                              dst=PC[0:65, hAB * 512:hAB * 512 + 512],
                              lh=bv(o_vh[slot] + cc * 132 + coloff, 65),
                              r=bv(pto + ni * 512, 512),
                              st=(first and cc == 0 and ni == 0),
                              sp=(last and ni == 1):
                              e.matmul(dst, lh, r, start=st, stop=sp),
                              waits=w,
                              inc=pe if ni == 1 else None)
                        PT_war[hAB][ra] = pe.now
                pv_done = pe.now
                if g == 0 and p + 1 < NPAIR:
                    ns0 = (2 * p + 2) % 3
                    kth_war[ns0] = pv_p_end
                    vh_war[ns0] = pv_p_end
                    load_kv(p + 1, 0, ns0)
                if g == OWN_QB - 1:
                    pv_p_end = pv_done
                    if p + 1 < NPAIR:
                        ns1 = (2 * p + 3) % 3
                        kth_war[ns1] = pv_done
                        vh_war[ns1] = pv_done
                        load_kv(p + 1, 1, ns1)

                dcast = V(lambda e: e.tensor_copy(bv(o_recip, 1024)[64:65, :],
                                                  PC[64:65, :]), waits=[pv_done])
                if debug and stage == 2:
                    G(lambda e, d=dbg["l"][p][g:g + 1, :],
                      s=bv(o_recip, 1024)[64:65, :]: e.dma_start(d, s),
                      waits=[dcast])
                T(lambda e: e.matmul(PEx[0:64, 0:512], ONES[64:65, 0:64],
                                     bv(o_recip, 512)[64:65, :], start=True, stop=True),
                  waits=[dcast, bc2_war], inc=pe)
                T(lambda e: e.matmul(PEx[0:64, 512:1024], ONES[64:65, 0:64],
                                     bv(o_recip + 512, 512)[64:65, :], start=True, stop=True),
                  inc=pe)
                nbd = pe.now
                rcp = V(lambda e: e.reciprocal(af[0:64, o_bAf:o_bAf + 1024],
                                               PEx[0:64, :]), waits=[nbd])
                bc2_war = rcp
                V(lambda e, d=bv(o_O + p * TOK + g * 512, 512)[0:64, :]:
                  e.tensor_mul(d, PC[0:64, 0:512], af[0:64, o_bAf:o_bAf + 512]))
                sgi = g % 2
                mulB = V(lambda e, d=bv(o_stgB[sgi], 512)[0:64, :]:
                         e.tensor_mul(d, PC[0:64, 512:1024],
                                      af[0:64, o_bAf + 512:o_bAf + 1024]),
                         waits=[stg_war[sgi]])
                acc_war = vch.now
                SY(lambda e, d=bv(o_O + p * TOK + g * 512, 512)[64:128, :],
                   s_=bv(o_stgB[sgi], 512)[0:64, :]: e.dma_start(d, s_), waits=[mulB])
                stg_war[sgi] = sq.now

        p2_v = vch.now
        p2_sq = sq.now
        p2_pe = pe.now
        p2_s = sch.now

        if debug and stage == 2:
            for oc in range(8):
                G(lambda e, d=dbg["O"][oc * 128:(oc + 1) * 128, :],
                  s=bv(o_O + oc * TOK, TOK): e.dma_start(d, s), waits=[p2_v, p2_sq])
        if stage == 2:
            plan.op("sync", None, waits=[gq.now, sq.now])
            _emit(nc, es, plan)
            return nc

        proj_war = [p2_s, p2_s]
        fA_war = [NONE, NONE]
        fB_war = [NONE, NONE]
        fC_war = [NONE, NONE]
        pc_war = p2_v
        pex_war = bc2_war
        gate_free = NONE
        x2_writes = 0

        def fa(i):
            return af[:, o_fA + i * D:o_fA + (i + 1) * D]

        def fb(i):
            return af[:, o_fB + i * D:o_fB + (i + 1) * D]

        def fc(i):
            return af[:, o_fC + i * D:o_fC + (i + 1) * D]

        for Ti in range(16):
            g, tt = Ti // 4, Ti % 4
            ri = Ti % 2
            ps = PA if ri == 0 else PB
            xown_ld = SY(lambda e, d=fa(ri), s=x_own[Ti * 128:(Ti + 1) * 128, :]:
                         e.dma_start(d, s), waits=[fA_war[ri]])
            for cc in range(8):
                for nb in range(2):
                    T(lambda e, dst=ps[:, nb * 512:(nb + 1) * 512],
                      lh=bv(o_O + cc * TOK + g * 512 + tt * 128, 128),
                      r=bv(o_woT + cc * D + nb * 512, 512),
                      st=(cc == 0), sp=(cc == 7): e.matmul(dst, lh, r, start=st, stop=sp),
                      waits=[p2_v, p2_sq, proj_war[ri], prolog_gq] if (cc == 0 and nb == 0) else [],
                      inc=pe if (cc == 7 and nb == 1) else None)
            prj = pe.now
            x2v = V(lambda e, d=fb(ri), s_=ps, a=fa(ri): e.scalar_tensor_tensor(
                d, s_[:, :], 1.0, a, ALU.mult, ALU.add),
                waits=[prj, xown_ld, fB_war[ri]])
            proj_war[ri] = x2v
            fA_war[ri] = x2v
            SY(lambda e, d=x2_hbm[Ti * 128:(Ti + 1) * 128, :], s_=fb(ri): e.dma_start(d, s_),
               waits=[x2v])
            x2_writes = sq.count
            fB_war[ri] = sq.now

            if P3_SUB < 2:
                continue
            c_s, c_m, c_ss, c_var, c_sd, c_rstd = (cols[:, i:i + 1] for i in range(6))
            V(lambda e, s_=fb(ri): e.reduce_sum(c_s, s_, axis=AX.X), waits=[x2v])
            S(lambda e, s_=fb(ri): e.activation(bv(o_sq3, 1024), s_, AF.Square),
              waits=[x2v] + ([(pe, p2_pe[1])] if Ti == 0 else []))
            V(lambda e: e.reduce_sum(c_ss, bv(o_sq3, 1024), axis=AX.X), waits=[sch.now])
            V(lambda e: e.tensor_scalar(c_m, c_s, 1.0 / D, None, ALU.mult))
            V(lambda e: e.tensor_scalar(c_ss, c_ss, 1.0 / D, None, ALU.mult))
            V(lambda e: e.tensor_mul(c_var, c_m, c_m))
            V(lambda e: e.tensor_sub(c_var, c_ss, c_var))
            V(lambda e: e.tensor_scalar(c_var, c_var, float(EPS), None, ALU.add))
            S(lambda e: e.activation(c_sd, c_var, AF.Sqrt), waits=[vch.now])
            V(lambda e: e.reciprocal(c_rstd, c_sd), waits=[sch.now])
            if P3_SUB < 3:
                continue
            xh2tok = V(lambda e, d=fc(ri), s_=fb(ri): e.tensor_scalar(
                d, s_, c_m, c_rstd, ALU.subtract, ALU.mult), waits=[fC_war[ri]])
            for kc in range(8):
                w = [xh2tok]
                if Ti == 0 and kc < 2:
                    w.append(pc_war)
                if kc >= 2:
                    w.append(vch.now)
                T(lambda e, dst=PC[:, (kc % 2) * 512:(kc % 2) * 512 + 128],
                  s_=fc(ri)[:, kc * 128:(kc + 1) * 128]:
                  e.transpose(dst, s_, ident_f[:, :]), waits=w, inc=pe)
                trk = pe.now
                V(lambda e, d=bv(o_xh2T + kc * TOK + Ti * 128, 128),
                  s_=PC[:, (kc % 2) * 512:(kc % 2) * 512 + 128]: e.tensor_copy(d, s_),
                  waits=[trk])
            fC_war[ri] = vch.now
            xh2T_done = vch.now
            if P3_SUB < 4:
                continue
            for kc in range(8):
                w = [xh2T_done] if kc == 0 else []
                if kc == 0:
                    w.append(gate_free)
                    if Ti == 0:
                        w.append(pex_war)
                T(lambda e, lh=bv(o_xh2T + kc * TOK + Ti * 128, 128),
                  r=bv(o_gwT + kc * 8, 8), st=(kc == 0), sp=(kc == 7):
                  e.matmul(PEx[:, 0:8], lh, r, start=st, stop=sp),
                  waits=w, inc=pe if kc == 7 else None)
            lg = pe.now
            if P3_SUB < 5:
                continue
            cmb = comb_sb[:, Ti * 8:(Ti + 1) * 8]
            mcol = cols[:, 6:7]
            V(lambda e: e.reduce_max(mcol, PEx[:, 0:8], axis=AX.X, negate=True), waits=[lg])
            S(lambda e: e.activation(t8a[:, :], PEx[:, 0:8], AF.Exp, bias=mcol), waits=[vch.now])
            gate_free = sch.now
            V(lambda e: e.reduce_sum(cols[:, 7:8], t8a[:, :], axis=AX.X), waits=[sch.now])
            V(lambda e: e.reciprocal(cols[:, 8:9], cols[:, 7:8]))
            V(lambda e: e.tensor_scalar(t8a[:, :], t8a[:, :], cols[:, 8:9], None, ALU.mult))
            V(lambda e: e.memset(t8b[:, :], 0.0))
            for ei in range(8):
                V(lambda e, c=t8a[:, ei:ei + 1]: e.tensor_scalar(
                    t8c[:, :], t8a[:, :], c, None, ALU.is_gt))
                V(lambda e, d=t8b[:, ei:ei + 1]: e.reduce_sum(d, t8c[:, :], axis=AX.X))
                if ei > 0:
                    V(lambda e, c=t8a[:, ei:ei + 1], s_=t8a[:, 0:ei], d=t8c[:, 0:ei]:
                      e.tensor_scalar(d, s_, c, None, ALU.is_equal))
                    V(lambda e, s_=t8c[:, 0:ei]: e.reduce_sum(cols[:, 9:10], s_, axis=AX.X))
                    V(lambda e, d=t8b[:, ei:ei + 1]: e.tensor_add(d, d, cols[:, 9:10]))
            V(lambda e: e.tensor_scalar(t8c[:, :], t8b[:, :], 2.0, None, ALU.is_lt))
            V(lambda e, d=cmb: e.tensor_mul(d, t8a[:, :], t8c[:, :]))

        p3_v = vch.now
        p3_sq = sq.now
        p3_pe = pe.now
        p3_s = sch.now

        if debug and stage == 3:
            for kc in range(8):
                G(lambda e, d=dbg["xh2T"][kc * 128:(kc + 1) * 128, :],
                  s=bv(o_xh2T + kc * TOK, TOK): e.dma_start(d, s), waits=[p3_v])
            G(lambda e, d=dbg["comb"].rearrange("t p e -> p t e"),
              s=comb_sb[:, :].rearrange("p (t e) -> p t e", e=8): e.dma_start(d, s),
              waits=[p3_v])
        if stage == 3:
            plan.op("sync", None, waits=[gq.now, sq.now])
            _emit(nc, es, plan)
            return nc

        w1c_war = [p3_pe, p3_pe]
        w3c_war = [p3_pe, p3_pe]
        w2h_war = [p3_pe, p3_pe]
        st_war = [p3_pe, p3_pe]
        h1_war = [p3_v, p3_v]
        h3_war = [p3_v, p3_v]
        hhat_war = [p3_pe] * EHC
        yps_war = [p3_v, p3_s]
        yacc_war = [NONE] * 8

        w1v = w1.rearrange("e (c p) j -> e p c j", p=128)
        w3v = w3.rearrange("e (c p) j -> e p c j", p=128)
        sw1v = sw1.rearrange("(c p) j -> p c j", p=128)
        sw3v = sw3.rearrange("(c p) j -> p c j", p=128)

        first_ffn = [True]

        def ffn_branch(pr, njc, w1s, w3s, w2s, comb_fn):
            nonlocal w1c_war, w3c_war, w2h_war, st_war, h1_war, h3_war, hhat_war, yps_war, yacc_war
            tq = pr * 1024
            w1ld = w3ld = None
            for jc in range(njc):
                if jc % 2 == 0:
                    wb = (jc // 2) % 2
                    n2 = min(2, njc - jc) * 128
                    w1ld = G(lambda e,
                             d=bv(o_w1c[wb], 8 * n2).rearrange("p (c j) -> p c j", c=8),
                             s=w1s[:, :, jc * 128:jc * 128 + n2]: e.dma_start(d, s),
                             waits=[w1c_war[wb]])
                    w3ld = G(lambda e,
                             d=bv(o_w3c[wb], 8 * n2).rearrange("p (c j) -> p c j", c=8),
                             s=w3s[:, :, jc * 128:jc * 128 + n2]: e.dma_start(d, s),
                             waits=[w3c_war[wb]])
                wb = (jc // 2) % 2
                n2 = min(2, njc - (jc // 2) * 2) * 128
                pair_last = (jc % 2 == 1) or (jc == njc - 1)
                for th in range(2):
                    it = jc * 2 + th
                    hb = it % 2
                    h1ps = PA[:, hb * 512:(hb + 1) * 512]
                    h3ps = PB[:, hb * 512:(hb + 1) * 512]
                    for kc in range(8):
                        w = []
                        if kc == 0:
                            w = [w1ld, h1_war[hb]]
                            if first_ffn[0]:
                                w.append(p3_v)
                        T(lambda e, dst=h1ps,
                          lh=bv(o_w1c[wb] + kc * n2 + (jc % 2) * 128, 128),
                          r=bv(o_xh2T + kc * TOK + tq + th * 512, 512),
                          st=(kc == 0), sp=(kc == 7):
                          e.matmul(dst, lh, r, start=st, stop=sp),
                          waits=w, inc=pe if kc == 7 else None)
                    first_ffn[0] = False
                    h1g = pe.now
                    if pair_last and th == 1:
                        w1c_war[wb] = h1g
                    for kc in range(8):
                        T(lambda e, dst=h3ps,
                          lh=bv(o_w3c[wb] + kc * n2 + (jc % 2) * 128, 128),
                          r=bv(o_xh2T + kc * TOK + tq + th * 512, 512),
                          st=(kc == 0), sp=(kc == 7):
                          e.matmul(dst, lh, r, start=st, stop=sp),
                          waits=[w3ld, h3_war[hb]] if kc == 0 else [],
                          inc=pe if kc == 7 else None)
                    h3g = pe.now
                    if pair_last and th == 1:
                        w3c_war[wb] = h3g
                    sb = it % 2
                    sdone = S(lambda e, d=bv(o_stile[sb], 512), s_=h1ps: e.activation(
                        d, s_, AF.Silu), waits=[h1g, st_war[sb]])
                    h1_war[hb] = sdone
                    hev = V(lambda e, d=bv(o_hh + jc * 1024 + th * 512, 512), s_=h3ps,
                            t_=bv(o_stile[sb], 512):
                            e.tensor_mul(d, s_, t_),
                            waits=[h3g, sdone, hhat_war[jc]])
                    h3_war[hb] = hev
                    st_war[sb] = hev
            hh_all = vch.now
            per = (njc + 1) // 2
            for jb in range(2):
                lo = jb * per
                hi = min(njc, lo + per)
                mid = lo + (hi - lo) // 2
                slab = bv(o_w2h[jb], (hi - lo) * D).rearrange("p (c j) -> p c j",
                                                              c=hi - lo)
                w2a = G(lambda e, d=slab[:, 0:mid - lo, :],
                        s=w2s[lo * 128:mid * 128, :].rearrange("(c p) j -> p c j", p=128):
                        e.dma_start(d, s), waits=[w2h_war[jb]])
                w2b = G(lambda e, d=slab[:, mid - lo:hi - lo, :],
                        s=w2s[mid * 128:hi * 128, :].rearrange("(c p) j -> p c j", p=128):
                        e.dma_start(d, s), waits=[])
                for tt in range(8):
                    Ti = pr * 8 + tt
                    yr = tt % 2
                    yps = PC if yr == 0 else PEx
                    for jj in range(hi - lo):
                        jc = lo + jj
                        for nb in range(2):
                            T(lambda e, dst=yps[:, nb * 512:(nb + 1) * 512],
                              lh=bv(o_hh + jc * 1024 + (tt // 4) * 512 + (tt % 4) * 128, 128),
                              r=bv(o_w2h[jb] + jj * D + nb * 512, 512),
                              st=(jj == 0), sp=(jj == hi - lo - 1):
                              e.matmul(dst, lh, r, start=st, stop=sp),
                              waits=[w2a, w2b, hh_all, yps_war[yr]] if (jj == 0 and nb == 0) else [],
                              inc=pe if (jj == hi - lo - 1 and nb == 1) else None)
                    ygrp = pe.now
                    yps_war[yr] = V(lambda e,
                                    d=af[:, o_yacc + tt * D:o_yacc + (tt + 1) * D],
                                    s_=yps, c=comb_fn(Ti):
                                    e.scalar_tensor_tensor(d, s_[:, :], c, d, ALU.mult, ALU.add),
                                    waits=[ygrp, yacc_war[tt]])
                    yacc_war[tt] = yps_war[yr]
                w2h_war[jb] = pe.now
            for jc in range(njc):
                hhat_war[jc] = pe.now

        def dump(dst, src_off, n):
            dv = G(lambda e, d=dst, s=bv(src_off, n): e.dma_start(d, s),
                   waits=[vch.now])
            plan.op("vector", None, waits=[dv])
            plan.op("tensor", None, waits=[dv])

        def dumpf(dst, src_off, n):
            dv = G(lambda e, d=dst, s=af[:, src_off:src_off + n]: e.dma_start(d, s),
                   waits=[vch.now])
            plan.op("vector", None, waits=[dv])
            plan.op("tensor", None, waits=[dv])

        out_dmas = 0
        f4 = [NONE, NONE]
        for pr in range(2):
            for tt in range(8):
                w = [yacc_war[tt]]
                if pr == 0:
                    w.append(p3_pe)
                yacc_war[tt] = V(lambda e, d=af[:, o_yacc + tt * D:o_yacc + (tt + 1) * D]:
                                 e.memset(d, 0.0), waits=w)
            for ei in range(EXPERTS):
                ffn_branch(pr, EHC, w1v[ei], w3v[ei], w2[ei],
                           (lambda ei=ei: (lambda Ti: comb_sb[:, Ti * 8 + ei:Ti * 8 + ei + 1]))())
                if debug and stage == 4 and pr == 0 and ei == 0:
                    dump(dbg["hhat"], o_hh, EHC * 1024)
                    dumpf(dbg["yaccE"], o_yacc, 8 * D)
            ffn_branch(pr, SHC, sw1v, sw3v, sw2, lambda Ti: 1.0)
            if debug and stage == 4 and pr == 0:
                dumpf(dbg["yaccS"], o_yacc, 8 * D)
            for tt in range(8):
                Ti = pr * 8 + tt
                ri = tt % 2
                x2ld = SY(lambda e, d=fa(ri), s=x2_hbm[Ti * 128:(Ti + 1) * 128, :]:
                          e.dma_start(d, s), waits=[(sq, x2_writes), fA_war[ri]])
                addv = V(lambda e, d=fb(ri), a=af[:, o_yacc + tt * D:o_yacc + (tt + 1) * D],
                         b=fa(ri): e.tensor_add(d, a, b),
                         waits=[x2ld, yacc_war[tt], fB_war[ri], f4[ri]])
                fA_war[ri] = addv
                yacc_war[tt] = addv
                SY(lambda e, d=out[Ti * 128:(Ti + 1) * 128, :], s_=fb(ri): e.dma_start(d, s_),
                   waits=[addv])
                fB_war[ri] = sq.now
                f4[ri] = sq.now

        plan.op("sync", None, waits=[sq.now, gq.now])
        _emit(nc, es, plan)
    return nc


def _emit(nc, es, plan):
    for s in plan.streams:
        s.sem = es.enter_context(nc.semaphore(s.name))
    block = es.enter_context(nc.Block())

    @block.tensor
    def _(t):
        plan.replay("tensor", t)

    @block.vector
    def _(v):
        plan.replay("vector", v)

    @block.scalar
    def _(s):
        plan.replay("scalar", s)

    @block.gpsimd
    def _(g):
        plan.replay("gpsimd", g)

    @block.sync
    def _(sy):
        plan.replay("sync", sy)


_ONES = np.ones((128, 128), dtype=ml_dtypes.bfloat16)
_IDENTF = np.eye(128, dtype=np.float32)


def make_in_maps(inputs):
    x = np.asarray(inputs["L0_attn_x"])
    base = {
        "wqT": np.ascontiguousarray(np.asarray(inputs["L0_attn_w_q"]).T),
        "wkT": np.ascontiguousarray(np.asarray(inputs["L0_attn_w_k"]).T),
        "wvT": np.ascontiguousarray(np.asarray(inputs["L0_attn_w_v"]).T),
        "woT": np.ascontiguousarray(np.asarray(inputs["L0_attn_w_o"]).T),
        "gwT": np.ascontiguousarray(np.asarray(inputs["L0_ffn_gate_w"]).T),
        "w1": np.ascontiguousarray(np.asarray(inputs["L0_ffn_w1"])),
        "w2": np.ascontiguousarray(np.asarray(inputs["L0_ffn_w2"])),
        "w3": np.ascontiguousarray(np.asarray(inputs["L0_ffn_w3"])),
        "sw1": np.ascontiguousarray(np.asarray(inputs["L0_ffn_sw1"])),
        "sw2": np.ascontiguousarray(np.asarray(inputs["L0_ffn_sw2"])),
        "sw3": np.ascontiguousarray(np.asarray(inputs["L0_ffn_sw3"])),
        "ones_in": _ONES,
        "identf_in": _IDENTF,
    }
    in_maps = []
    for c in range(8):
        b, m = c // 4, c % 4
        xr = np.roll(x[b], -TOK * m, axis=0)
        im = dict(base)
        im["xT"] = np.ascontiguousarray(xr.T)
        im["x_own"] = np.ascontiguousarray(xr[0:TOK])
        in_maps.append(im)
    return in_maps


def kernel(**inputs):
    nc = build_nc(stage=4, debug=False)
    in_maps = make_in_maps(inputs)
    res = run_bass_kernel_spmd(nc, in_maps, core_ids=list(range(8)))
    out = np.empty((2, SEQ, D), dtype=np.float32)
    for c in range(8):
        b, m = c // 4, c % 4
        out[b, m * TOK:(m + 1) * TOK, :] = res.results[c]["out"]
    return out
